# version 15
# speedup vs baseline: 1.1506x; 1.0003x over previous
import numpy as np
import contextlib
import concourse.bass as bass
import concourse.mybir as mybir
from concourse.bass_utils import run_bass_kernel_spmd

F32 = mybir.dt.float32
BF16 = mybir.dt.bfloat16
F16 = mybir.dt.float16
FP8 = mybir.dt.float8e4
AF = mybir.ActivationFunctionType
OP = mybir.AluOpType

f32 = np.float32
C52 = float(f32(1.5 * 2.0**52))
C32 = float(f32(1.5 * 2.0**23))
LN2_HI = float(f32(np.log(2.0)))
LN2_LO = float(f32(np.log(2.0) - np.float64(f32(np.log(2.0)))))
INV_LN2 = float(f32(1.0 / np.log(2.0)))
INV_LN2H = float(f32(1.0 / np.log(2.0))) / 2.0
TINY2 = float(f32(2e-38))
SQRT1_2 = float(f32(np.sqrt(0.5)))
TPH = float(f32(2 * np.pi))
TPL = float(f32(2 * np.pi - np.float64(f32(2 * np.pi))))
I2PH = float(f32(1 / (2 * np.pi)))
I2PL = float(f32(1 / (2 * np.pi) - np.float64(f32(1 / (2 * np.pi)))))
PI = float(f32(np.pi))
PIH = float(f32(np.pi / 2))
TINY = float(f32(1e-19))
EPS = float(f32(1e-5))
BH14 = float(f32(14 * np.float64(f32(np.log(2.0)))))
FL14 = float(f32(1.0 + 14 * np.float64(f32(np.log(2.0) - np.float64(f32(np.log(2.0)))))))
CS16 = float(f32(1.5 * 2.0 ** (52 - 14 - 10)))
CS8 = float(f32(1.5 * 2.0 ** (52 - 14 - 3)))
I1024 = float(f32(1.0 / 1024.0))

DM = 1024
T = 4096
GRP = 512
NCH = DM // 128
HF = 4096
NHF = HF // 128
PHB = 1024
HB = 512


def _threefry2x32(k0, k1, x0, x1):
    R0 = (13, 15, 26, 6); R1 = (17, 29, 16, 24)
    ks0 = np.uint32(k0); ks1 = np.uint32(k1)
    ks2 = (ks0 ^ ks1 ^ np.uint32(0x1BD11BDA))
    x0 = (x0 + ks0); x1 = (x1 + ks1)
    ks = (ks1, ks2, ks0)
    for i in range(5):
        rot = R0 if i % 2 == 0 else R1
        for r in rot:
            x0 = (x0 + x1)
            x1 = ((x1 << np.uint32(r)) | (x1 >> np.uint32(32 - r)))
            x1 = (x1 ^ x0)
        x0 = (x0 + ks[i % 3])
        x1 = (x1 + ks[(i + 1) % 3] + np.uint32(i + 1))
    return x0, x1


def host_masks():
    n = 8 * T * DM
    idx = np.arange(n, dtype=np.uint32)
    o0, o1 = _threefry2x32(0, 42, np.zeros(n, np.uint32), idx)
    b32 = o0 ^ o1
    u32 = (((b32 >> np.uint32(9)) | np.uint32(0x3F800000)).view(np.float32) - np.float32(1.0))
    mA = (u32 < np.float32(0.8)).reshape(8, T, DM)
    b64 = (o0.astype(np.uint64) << np.uint64(32)) | o1.astype(np.uint64)
    u64 = (((b64 >> np.uint64(12)) | np.uint64(0x3FF0000000000000)).view(np.float64) - 1.0)
    mB = (u64 < 0.8).reshape(8, T, DM)
    maskA = np.where(mA, np.float16(1.25), np.float16(0)).astype(np.float16)
    maskD = (np.where(mB, np.float32(1.25), np.float32(0))
             - maskA.astype(np.float32)).astype(np.float16)
    return maskA, maskD


def _foot(ap):
    try:
        t_ = ap.tensor
        shape = list(t_.shape)
        W = int(shape[-1])
        off = int(ap.offset)
        lo = off % W if W > 0 else 0
        ext = 0
        for p in ap.ap:
            st, ct = int(p[0]), int(p[1])
            if st == 0 or (W and st % W == 0):
                continue
            ext += (int(ct) - 1) * abs(st)
        return (t_.name, lo, lo + ext + 1)
    except Exception:
        return None


class Ctx:

    def __init__(self, nc, stack):
        self.nc = nc
        self.stack = stack
        self.counts = {}
        self.sems = {}
        for name in (["pe", "act", "dve", "ld", "out"]
                     + ["ldx%d" % i for i in range(8)]):
            self.sems[name] = stack.enter_context(nc.semaphore(name))
            self.counts[name] = 0
        self.bias_tiles = {}
        self.vc = {}
        self.rel = {n: {} for n in self.sems}
        self.lastw = {}
        self.lastr = {}
        self.ekmap = {}

    def reg_engine(self, key, eng):
        self.ekmap[id(eng)] = key
        self.engs = getattr(self, "engs", {})
        self.engs[key] = eng
        self.vc[key] = {}

    def sb(self, name, shape, dt=F32, stk=None):
        return (stk or self.stack).enter_context(self.nc.sbuf_tensor(name, shape, dt))

    def ps(self, name, shape, dt=F32):
        return self.stack.enter_context(self.nc.psum_tensor(name, shape, dt))

    def inc(self, inst, sem, n=1):
        return inst

    def _join(self, ek, sem, cnt):
        vc = self.vc[ek]
        rv = self.rel[sem].get(cnt)
        if rv:
            for k2, v2 in rv.items():
                if v2 > vc.get(k2, 0):
                    vc[k2] = v2
        if cnt > vc.get(sem, 0):
            vc[sem] = cnt

    def wait(self, eng, sem, cnt=None):
        cnt = self.counts[sem] if cnt is None else cnt
        ek = self.ekmap[id(eng)]
        if self.vc[ek].get(sem, 0) >= cnt:
            return
        eng.wait_ge(self.sems[sem], cnt)
        self._join(ek, sem, cnt)

    @staticmethod
    def _scan(store, nm, lo, hi, needs):
        for (a, b, s_, c_) in store.get(nm, ()):
            if a < hi and lo < b and c_ > needs.get(s_, 0):
                needs[s_] = c_

    @staticmethod
    def _put(store, nm, lo, hi, sem, cnt):
        lst = store.setdefault(nm, [])
        lst[:] = [e for e in lst
                  if not (lo <= e[0] and e[1] <= hi and e[2] == sem and e[3] <= cnt)]
        lst.append((lo, hi, sem, cnt))
        if len(lst) > 96:
            by = {}
            for a, b, s_, c_ in lst:
                A, B, C = by.get(s_, (a, b, c_))
                by[s_] = (min(A, a), max(B, b), max(C, c_))
            lst[:] = [(A, B, s_, C) for s_, (A, B, C) in by.items()]

    def gop(self, ek, sem, emit, reads, writes, ninc=1, skip_self_waw=False):
        eng = self.engs[ek]
        vc = self.vc[ek]
        needs = {}
        rfeet = [f for f in (_foot(ap) for ap in reads) if f]
        wfeet = [f for f in (_foot(ap) for ap in writes) if f]
        for nm, lo, hi in rfeet:
            self._scan(self.lastw, nm, lo, hi, needs)
        for nm, lo, hi in wfeet:
            self._scan(self.lastw, nm, lo, hi, needs)
            self._scan(self.lastr, nm, lo, hi, needs)
        if skip_self_waw:
            needs.pop(sem, None)
        for s_, c_ in sorted(needs.items()):
            if vc.get(s_, 0) < c_:
                eng.wait_ge(self.sems[s_], c_)
                self._join(ek, s_, c_)
        inst = emit()
        inst.then_inc(self.sems[sem], ninc)
        self.counts[sem] += ninc
        cnt = self.counts[sem]
        for nm, lo, hi in rfeet:
            self._put(self.lastr, nm, lo, hi, sem, cnt)
        for nm, lo, hi in wfeet:
            self._put(self.lastw, nm, lo, hi, sem, cnt)
        self.rel[sem][cnt] = dict(vc, **{sem: cnt})
        return inst

    def biasc(self, val):
        val = float(val)
        if val not in self.bias_tiles:
            tile = self.sb("biasc_%d" % len(self.bias_tiles), [128, 1])
            self.bias_tiles[val] = tile
        return self.bias_tiles[val][:]


def build(n_tok=T, debug=False):
    ngrp = n_tok // GRP
    nc = bass.Bass("TRN2", target_bir_lowering=False, debug=False)
    stack = contextlib.ExitStack()
    c = Ctx(nc, stack)
    v, s, t, sy = nc.vector, nc.scalar, nc.tensor, nc.sync

    dri = lambda n_, sh, dt=F32: nc.dram_tensor(n_, sh, dt, kind="ExternalInput").ap()
    xT = dri("xT", [DM, n_tok])
    Wk_d = dri("Wk", [DM, DM]); Wv_d = dri("Wv", [DM, DM]); Wo_d = dri("Wo", [DM, DM])
    w1T_d = dri("w1T", [DM, HF]); w2T_d = dri("w2T", [HF, DM])
    wmod_d = dri("wmod", [DM, 6 * DM])
    c_col = dri("c_col", [128, 8])
    f_col = dri("f_col", [128, 8])
    gp_col = dri("gp_col", [128, 8])
    gac_col = dri("gac_col", [128, 8])
    b1_col = dri("b1_col", [128, 32])
    b2_col = dri("b2_col", [128, 8])
    jb_in = dri("jb", [128, GRP])
    inv_in = dri("invs", [128, n_tok])
    mA_in = dri("maskA", [DM, n_tok], F16)
    mD_in = dri("maskD", [DM, n_tok], F16)
    yout = nc.dram_tensor("y", [DM, n_tok], F32, kind="ExternalOutput").ap()

    xT_r = xT.rearrange("(k p) n -> p k n", p=128)
    yout_r = yout.rearrange("(k p) n -> p k n", p=128)
    mA_r = mA_in.rearrange("(k p) n -> p k n", p=128)
    mD_r = mD_in.rearrange("(k p) n -> p k n", p=128)
    wk_r = Wk_d.rearrange("(k p) n -> p k n", p=128)
    wv_r = Wv_d.rearrange("(k p) n -> p k n", p=128)
    wo_r = Wo_d.rearrange("(k p) n -> p k n", p=128)
    w1_r = w1T_d.rearrange("(k p) n -> p k n", p=128)
    w2_r = w2T_d.rearrange("(k p) n -> p k n", p=128)
    wm_r = wmod_d.rearrange("(k p) n -> p k n", p=128)

    wk_sb = c.sb("wk_sb", [128, NCH * DM], BF16)
    wv_sb = c.sb("wv_sb", [128, NCH * DM], BF16)
    wo_sb = c.sb("wo_sb", [128, NCH * DM], BF16)
    w1q_sb = c.sb("w1q_sb", [128, NCH * HF], FP8)
    w2q_sb = c.sb("w2q_sb", [128, NHF * DM], FP8)
    xb0 = c.sb("xb0", [128, NCH * GRP], F32)
    h1_sb = c.sb("h1_sb", [128, NCH * GRP], BF16)
    mix_sb = c.sb("mix_sb", [128, NCH * GRP], BF16)
    h2q_sb = c.sb("h2q_sb", [128, NCH * GRP], FP8)
    r_sb = c.sb("r_sb", [128, NHF * GRP], FP8)
    inv_sb = c.sb("inv_sb", [128, GRP], F32)
    jb_sb = c.sb("jb_sb", [128, GRP], F32)
    ones_gb = c.sb("ones_gb", [128, GRP], BF16)
    mkA = c.sb("mkA", [128, GRP], F16)
    mkD = c.sb("mkD", [128, GRP], F16)
    mkA2 = c.sb("mkA2", [128, GRP], F16)
    mkD2 = c.sb("mkD2", [128, GRP], F16)

    modt = c.sb("modt", [128, 48], F32)
    gA = c.sb("gA", [128, 8], F32)
    gB = c.sb("gB", [128, 8], F32)
    gmA = c.sb("gmA", [128, 8], F32)
    gmB = c.sb("gmB", [128, 8], F32)
    b1q_sb = c.sb("b1q_sb", [128, 32], F32)
    b2q_sb = c.sb("b2q_sb", [128, 8], F32)
    f_sb = c.sb("f_sb", [128, 8], F32)
    gp_sb = c.sb("gp_sb", [128, 8], F32)
    gac_sb = c.sb("gac_sb", [128, 8], F32)
    c_sb = c.sb("c_sb", [128, 8], F32)
    t0f = c.sb("t0f", [128, 8], F32)
    e1c = c.sb("e1c", [128, 1], F32)
    dcol = c.sb("dcol", [128, 1], F32)
    ceff = c.sb("ceff", [128, 1], F32)
    nceff = c.sb("nceff", [128, 1], F32)
    tqc = c.sb("tqc", [128, 1], F32)
    ckb = c.sb("ckb", [128, 8], BF16)
    skb = c.sb("skb", [128, 8], BF16)
    carry = c.sb("carry", [128, 32], F32)

    ones_colf = c.sb("ones_colf", [128, 1], F32)
    ones_rowb = c.sb("ones_rowb", [1, 128], BF16)

    p_kr = c.ps("p_kr", [128, GRP])
    p_v = c.ps("p_v", [128, GRP])
    p_kr2 = c.ps("p_kr2", [128, GRP])
    p_v2 = c.ps("p_v2", [128, GRP])
    p_a = c.ps("p_a", [128, GRP])
    p_b = c.ps("p_b", [128, GRP])
    p_c = c.ps("p_c", [128, GRP])
    p_mod = c.ps("p_mod", [128, 48])
    p_sx = p_a[0:1, :]
    p_sq = p_b[0:1, :]

    for val in [PIH, EPS, TINY, TINY2, BH14, 0.0]:
        c.biasc(val)

    c.reg_engine("v", v)
    c.reg_engine("s", s)
    c.reg_engine("t", t)
    c.reg_engine("y", sy)

    def _aps(*xs):
        return [x for x in xs if hasattr(x, "tensor")]

    def vts(out, in0, s1_, s2_=None, op0=OP.mult, op1=None):
        if s2_ is None and op1 is None:
            return c.gop("v", "dve", lambda: nc.vector.tensor_scalar(
                out=out, in0=in0, scalar1=s1_, scalar2=None, op0=op0),
                _aps(in0, s1_), [out])
        return c.gop("v", "dve", lambda: nc.vector.tensor_scalar(
            out=out, in0=in0, scalar1=s1_, scalar2=s2_, op0=op0, op1=op1),
            _aps(in0, s1_, s2_), [out])

    def vtt(out, a, b_, op=OP.mult):
        return c.gop("v", "dve", lambda: nc.vector.tensor_tensor(
            out=out, in0=a, in1=b_, op=op), [a, b_], [out])

    def vstt(out, in0, sc, in1, op0=OP.mult, op1=OP.add):
        return c.gop("v", "dve", lambda: nc.vector.scalar_tensor_tensor(
            out=out, in0=in0, scalar=sc, in1=in1, op0=op0, op1=op1),
            _aps(in0, sc, in1), [out])

    def vcp(out, in_):
        return c.gop("v", "dve", lambda: nc.vector.tensor_copy(out=out, in_=in_),
                     [in_], [out])

    def vscan(out, data0, data1, initial, op0, op1):
        return c.gop("v", "dve", lambda: nc.vector.tensor_tensor_scan(
            out=out, data0=data0, data1=data1, initial=initial, op0=op0, op1=op1),
            [data0, data1, initial], [out])

    def vnop():
        return c.gop("v", "dve", lambda: nc.vector.engine_nop(), [], [])

    def sact(out, in_, func, scale=None, bias=None):
        kw = {}
        if scale is not None:
            kw["scale"] = scale
        if bias is not None:
            kw["bias"] = bias
        return c.gop("s", "act", lambda: nc.scalar.activation(
            out=out, in_=in_, func=func, **kw), _aps(in_, scale, bias), [out])

    def mm(out, lhsT, rhs, start, stop):
        return c.gop("t", "pe", lambda: nc.tensor.matmul(
            out=out, lhsT=lhsT, rhs=rhs, start=start, stop=stop),
            [lhsT, rhs], [out], skip_self_waw=True)

    def dma(dst, src, sem):
        return c.gop("y", sem, lambda: nc.sync.dma_start(dst, src),
                     [src], [dst], ninc=16)

    G = {}

    def emit_qchain(src, width, man, emin, emax, fill=None):
        q1, q2, q3, q4, q6 = G['q1'], G['q2'], G['q3'], G['q4'], G['q6']
        CS = CS8 if man == 3 else CS16
        fl = list(fill or [])
        def pad():
            f = fl.pop(0) if fl else None
            if f is not None:
                f()
        sl = lambda tile: tile[:, :width]
        vstt(sl(q1), src, -1.0, src, op0=OP.mult, op1=OP.max)
        c.wait(s, "dve")
        ia = sact(out=sl(q2), in_=sl(q1), func=AF.Ln, scale=SQRT1_2, bias=c.biasc(TINY))
        c.inc(ia, "act")
        c.wait(v, "act")
        vts(sl(q3), sl(q2), INV_LN2, C52, op0=OP.mult, op1=OP.add)
        pad()
        vts(sl(q1), sl(q3), C52, float(emin), op0=OP.subtract, op1=OP.max)
        pad()
        iv = vts(sl(q3), sl(q1), float(emax), None, op0=OP.min)
        c.inc(iv, "dve")
        c.wait(s, "dve")
        ia = sact(out=sl(q2), in_=sl(q3), func=AF.Exp, scale=LN2_HI, bias=c.biasc(BH14))
        c.inc(ia, "act")
        c.wait(v, "act")
        vts(sl(q4), sl(q2), C52, C52, op0=OP.add, op1=OP.subtract)
        pad()
        vts(sl(q2), sl(q4), CS, None, op0=OP.mult)
        pad()
        vtt(sl(q1), src, sl(q2), op=OP.add)
        pad()
        vtt(sl(q6), sl(q1), sl(q2), op=OP.subtract)
        pad()

    def emit_blend(cast_ap, dst_ap, width):
        q4, q6 = G['q4'], G['q6']
        vtt(q4[:, :width], q6[:, :width], cast_ap, op=OP.subtract)
        iv = vstt(dst_ap, q4[:, :width], dcol[:], cast_ap, op0=OP.mult, op1=OP.add)
        return iv

    pstk = contextlib.ExitStack()
    wstage = c.sb("wstage", [128, HF], F32, stk=pstk)
    c8s = c.sb("c8s", [128, PHB], FP8, stk=pstk)
    pq1 = c.sb("pq1", [128, PHB], BF16, stk=pstk)
    pq2 = c.sb("pq2", [128, PHB], BF16, stk=pstk)
    pq3 = c.sb("pq3", [128, PHB], BF16, stk=pstk)
    pq4 = c.sb("pq4", [128, PHB], BF16, stk=pstk)
    pq6 = c.sb("pq6", [128, PHB], BF16, stk=pstk)
    silu_bf = c.sb("silu_bf", [128, 8], F32, stk=pstk)
    mrow = c.sb("mrow", [2, 512], F32, stk=pstk)
    sm1 = c.sb("sm1", [128, 8], F32, stk=pstk)
    sm2 = c.sb("sm2", [128, 8], F32, stk=pstk)
    sm3 = c.sb("sm3", [128, 8], F32, stk=pstk)
    sm4 = c.sb("sm4", [128, 8], F32, stk=pstk)
    G.update(q1=pq1, q2=pq2, q3=pq3, q4=pq4, q6=pq6)

    dma(c_sb[:], c_col, "ld")
    dma(f_sb[:], f_col, "ld")
    dma(gp_sb[:], gp_col, "ld")
    dma(gac_sb[:], gac_col, "ld")
    dma(jb_sb[:], jb_in, "ld")
    dma(b1q_sb[:], b1_col, "ld")
    dma(b2q_sb[:], b2_col, "ld")
    ld_small = c.counts["ld"]

    c.wait(v, "ld", ld_small)
    vts(ones_colf[:], c_sb[:, 0:1], 0.0, 1.0, op0=OP.mult, op1=OP.add)
    vts(ones_rowb[:], jb_sb[0:1, 0:128], 0.0, 1.0, op0=OP.mult, op1=OP.add)
    vts(ones_gb[:], jb_sb[:], 0.0, 1.0, op0=OP.mult, op1=OP.add)
    vts(carry[:], jb_sb[:, 0:32], 0.0, None, op0=OP.mult)
    vts(ckb[:], c_sb[:], 0.0, None, op0=OP.mult)
    vts(skb[:], c_sb[:], 0.0, None, op0=OP.mult)
    for val, tile in c.bias_tiles.items():
        vts(tile[:], c_sb[:, 0:1], 0.0, float(val), op0=OP.mult, op1=OP.add)
    pass
    vts(e1c[:], ones_colf[:], float(2.0 ** -25), None, op0=OP.add)
    pass
    vts(dcol[:], e1c[:], 1.0, float(2.0 ** 25), op0=OP.subtract, op1=OP.mult)
    pass
    vts(tqc[:], dcol[:], C52, None, op0=OP.mult)
    vts(ceff[:], dcol[:], -C32, C32, op0=OP.mult, op1=OP.add)
    pass
    iv = vtt(ceff[:], ceff[:], tqc[:], op=OP.add)
    c.inc(iv, "dve")
    vts(nceff[:], ceff[:], -1.0, None, op0=OP.mult)

    for wd_r, wsb in [(wk_r, wk_sb), (wv_r, wv_sb), (wo_r, wo_sb)]:
        for half in range(2):
            c.wait(sy, "dve")
            for kq in range(4):
                kc = half * 4 + kq
                dma(wstage[:, kq * DM:(kq + 1) * DM], wd_r[:, kc, :], "ld")
            c.wait(v, "ld")
            iv = vcp(out=wsb[:, half * 4 * DM:(half + 1) * 4 * DM],
                               in_=wstage[:, :4 * DM])
            c.inc(iv, "dve")

    c.wait(s, "ld", ld_small)
    c.wait(s, "dve")
    sact(out=silu_bf[:], in_=c_sb[:], func=AF.Silu)
    c.wait(v, "ld", ld_small)
    vts(mrow[:], jb_sb[0:2, :], 0.0, None, op0=OP.mult)
    for jblk in range(0, 48, 4):
        c.wait(sy, "ld", c.counts["ld"])
        for kc in range(NCH):
            dma(wstage[:, kc * 512:(kc + 1) * 512],
                wm_r[:, kc, jblk * 128:(jblk + 4) * 128], "ld")
        c.wait(t, "ld")
        for kc in range(NCH):
            mm(out=p_a[0:1, :], lhsT=silu_bf[:, kc:kc + 1],
               rhs=wstage[:, kc * 512:(kc + 1) * 512],
               start=(kc == 0), stop=(kc == NCH - 1))
        vcp(out=mrow[0:1, :], in_=p_a[0:1, :])
        for jc in range(4):
            mm(out=p_mod[:, jblk + jc:jblk + jc + 1],
               lhsT=mrow[0:2, jc * 128:(jc + 1) * 128],
               rhs=ones_colf[0:2, :], start=True, stop=True)
    c.wait(v, "pe")
    vcp(out=modt[:], in_=p_mod[:])
    pass

    ga8 = modt[:, 16:24]; gb8 = modt[:, 40:48]
    vts(sm1[:], ga8, 0.0, None, op0=OP.is_ge)
    vstt(sm3[:], ga8, -1.0, ga8, op0=OP.mult, op1=OP.max)
    pass
    vts(sm2[:], sm1[:], 2.0, -1.0, op0=OP.mult, op1=OP.add)
    vts(sm4[:], sm3[:], 0.25, None, op0=OP.max)
    pass
    vtt(gA[:], sm2[:], sm4[:], op=OP.mult)
    vts(sm1[:], gb8, 0.0, None, op0=OP.is_ge)
    vstt(sm3[:], gb8, -1.0, gb8, op0=OP.mult, op1=OP.max)
    pass
    vts(sm2[:], sm1[:], 2.0, -1.0, op0=OP.mult, op1=OP.add)
    vts(sm4[:], sm3[:], 0.25, None, op0=OP.max)
    pass
    vtt(gB[:], sm2[:], sm4[:], op=OP.mult)
    vts(gmA[:], modt[:, 8:16], 1.0, None, op0=OP.add)
    iv = vts(gmB[:], modt[:, 32:40], 1.0, None, op0=OP.add)
    c.inc(iv, "dve")
    pass

    for bsb, wdt in [(b1q_sb, 32), (b2q_sb, 8)]:
        iv = vcp(out=c8s[:, :wdt], in_=bsb[:, :wdt])
        c.inc(iv, "dve")
        emit_qchain(bsb[:, :wdt], wdt, 3, -6.0, 6.0)
        iv = emit_blend(c8s[:, :wdt], bsb[:, :wdt], wdt)
        c.inc(iv, "dve")

    def emit_wpair(srcA, srcB, dstA, dstB):
        W2 = PHB // 2
        q1A, q1B = pq1[:, :W2], pq1[:, W2:PHB]
        q2A, q2B = pq2[:, :W2], pq2[:, W2:PHB]
        q3A, q3B = pq3[:, :W2], pq3[:, W2:PHB]
        q4A, q4B = pq4[:, :W2], pq4[:, W2:PHB]
        q6A, q6B = pq6[:, :W2], pq6[:, W2:PHB]
        c8A, c8B = c8s[:, :W2], c8s[:, W2:PHB]
        sact(out=q1A, in_=srcA, func=AF.Square)
        sact(out=q1B, in_=srcB, func=AF.Square)
        sact(out=q2A, in_=q1A, func=AF.Ln, scale=0.5, bias=c.biasc(TINY2))
        lnA = c.counts["act"]
        sact(out=q2B, in_=q1B, func=AF.Ln, scale=0.5, bias=c.biasc(TINY2))
        lnB = c.counts["act"]
        vcp(out=c8A, in_=srcA)
        vcp(out=c8B, in_=srcB)
        c.wait(v, "act", lnA)
        vts(q3A, q2A, INV_LN2H, C52, op0=OP.mult, op1=OP.add)
        c.wait(v, "act", lnB)
        vts(q3B, q2B, INV_LN2H, C52, op0=OP.mult, op1=OP.add)
        vts(q1A, q3A, C52, -6.0, op0=OP.subtract, op1=OP.max)
        vts(q1B, q3B, C52, -6.0, op0=OP.subtract, op1=OP.max)
        iv = vts(q3A, q1A, 6.0, None, op0=OP.min)
        eA = c.counts["dve"]
        iv = vts(q3B, q1B, 6.0, None, op0=OP.min)
        eB = c.counts["dve"]
        c.wait(s, "dve", eA)
        sact(out=q2A, in_=q3A, func=AF.Exp, scale=LN2_HI, bias=c.biasc(BH14))
        exA = c.counts["act"]
        c.wait(s, "dve", eB)
        sact(out=q2B, in_=q3B, func=AF.Exp, scale=LN2_HI, bias=c.biasc(BH14))
        exB = c.counts["act"]
        c.wait(v, "act", exA)
        vts(q1A, q2A, C52, C52, op0=OP.add, op1=OP.subtract)
        c.wait(v, "act", exB)
        vts(q1B, q2B, C52, C52, op0=OP.add, op1=OP.subtract)
        vts(q4A, q1A, CS8, None, op0=OP.mult)
        vts(q4B, q1B, CS8, None, op0=OP.mult)
        vtt(q2A, srcA, q4A, op=OP.add)
        vtt(q2B, srcB, q4B, op=OP.add)
        vtt(q6A, q2A, q4A, op=OP.subtract)
        vtt(q6B, q2B, q4B, op=OP.subtract)
        vtt(q4A, q6A, c8A, op=OP.subtract)
        vtt(q4B, q6B, c8B, op=OP.subtract)
        vstt(dstA, q4A, dcol[:], c8A, op0=OP.mult, op1=OP.add)
        vstt(dstB, q4B, dcol[:], c8B, op0=OP.mult, op1=OP.add)

    vchunks = ([(w1_r, w1q_sb, kc, h0) for kc in range(NCH) for h0 in range(0, HF, PHB)]
               + [(w2_r, w2q_sb, kc, 0) for kc in range(NHF)])
    for i, (wr, wqsb, kc, h0) in enumerate(vchunks):
        half = (i % 2) * PHB
        c.wait(sy, "ld", c.counts["ld"])
        dma(wstage[:, half:half + PHB], wr[:, kc, h0:h0 + PHB], "ld")
        ldw = c.counts["ld"]
        c.wait(v, "ld", ldw)
        wid = HF if wqsb is w1q_sb else DM
        base = kc * wid + h0
        emit_wpair(wstage[:, half:half + PHB // 2],
                   wstage[:, half + PHB // 2:half + PHB],
                   wqsb[:, base:base + PHB // 2],
                   wqsb[:, base + PHB // 2:base + PHB])

    c.wait(sy, "dve")
    pstk.close()

    ckt = c.sb("ckt", [128, GRP + 4], BF16)
    skt = c.sb("skt", [128, GRP + 4], BF16)
    S1 = c.sb("S1", [128, GRP], F32)
    S2 = c.sb("S2", [128, GRP], F32)
    S3 = c.sb("S3", [128, GRP], F32)
    scA = c.sb("scA", [128, GRP], F32)
    scB = c.sb("scB", [128, GRP], F32)
    mxa = c.sb("mxa", [128, GRP], F32)
    mxb = c.sb("mxb", [128, GRP], F32)
    hr_sb = c.sb("hr_sb", [128, HB], F32)
    h16_sb = c.sb("h16_sb", [128, 2 * HB], F16)
    gq3 = c.sb("gq3", [128, HB], BF16)
    gq4 = c.sb("gq4", [128, HB], BF16)
    gq6 = c.sb("gq6", [128, HB], BF16)
    gq3b = c.sb("gq3b", [128, HB], BF16)
    gq4b = c.sb("gq4b", [128, HB], BF16)
    gq6b = c.sb("gq6b", [128, HB], BF16)
    gq1 = c.sb("gq1", [128, HB], BF16)
    gq2 = c.sb("gq2", [128, HB], BF16)
    gq1b = c.sb("gq1b", [128, HB], BF16)
    gq2b = c.sb("gq2b", [128, HB], BF16)
    rowS = S3[0:1, :]
    rowV = mxa[0:1, :]
    rowT = mxb[0:1, :]
    rowAb = gq1b[0:1, :GRP]
    rowBb = gq2b[0:1, :GRP]
    G.update(q1=gq1, q2=gq2, q3=gq3, q4=gq4, q6=gq6)
    R1 = mxa; R2 = mxb
    R3v = hr_sb[:, :GRP]
    crt = gq3[:, :GRP]
    srt = h16_sb[:, 0:GRP]
    v_bf = h16_sb[:, GRP:2 * GRP]
    It0 = gq4[:, :GRP]
    It1 = gq6[:, :GRP]

    hrfree = {0: 0, 1: 0}
    mxfree = {0: 0, 1: 0}
    prev_out = {}
    prev_scB_free = 0
    prev_scA_free = 0
    for grp in range(ngrp):
        xv = lambda kc: xb0[:, kc * GRP:(kc + 1) * GRP]
        t0 = float(grp * GRP)
        gsl = slice(grp * GRP, (grp + 1) * GRP)

        c.wait(sy, "dve")
        dma(inv_sb[:], inv_in[:, gsl], "ld")
        ldx = {}
        for kc in range(NCH):
            sx = "ldx%d" % kc
            c.wait(sy, sx, prev_out.get(kc, 0))
            dma(xb0[:, kc * GRP:(kc + 1) * GRP], xT_r[:, kc, gsl], sx)
            ldx[kc] = c.counts[sx]

        c.wait(v, "ld")
        vts(t0f[:], f_sb[:], t0, None, op0=OP.mult)
        peB0 = c.counts["pe"]
        smm = {}
        for kc in range(NCH):
            sqt = scB if kc % 2 == 0 else scA
            c.wait(s, "dve", prev_scB_free if kc % 2 == 0 else prev_scA_free)
            c.wait(s, "ldx%d" % kc, ldx[kc])
            c.wait(s, "pe", smm[kc - 2] if kc >= 2 else peB0)
            ia = sact(out=sqt[:], in_=xv(kc), func=AF.Square)
            c.inc(ia, "act")
            c.wait(t, "act")
            c.wait(t, "ldx%d" % kc, ldx[kc])
            mm(out=p_sx, lhsT=ones_colf[:], rhs=xv(kc),
                     start=(kc == 0), stop=(kc == NCH - 1))
            im = mm(out=p_sq, lhsT=ones_colf[:], rhs=sqt[:],
                          start=(kc == 0), stop=(kc == NCH - 1))
            c.inc(im, "pe")
            smm[kc] = c.counts["pe"]
        c.wait(v, "pe")
        vts(rowS, p_sx, I1024, None, op0=OP.mult)
        pass
        vtt(rowT, rowS, rowS, op=OP.mult)
        pass
        iv = vstt(rowV, p_sq, I1024, rowT, op0=OP.mult, op1=OP.subtract)
        c.inc(iv, "dve")
        pass
        c.wait(s, "dve")
        ia = sact(out=rowT, in_=rowV, func=AF.Ln, bias=c.bias_tiles[EPS][0:1, :])
        c.inc(ia, "act")
        pass
        ia = sact(out=rowV, in_=rowT, func=AF.Exp, scale=-0.5)
        c.inc(ia, "act")
        c.wait(v, "act")
        vstt(rowS, rowS, -1.0, rowV, op0=OP.mult, op1=OP.mult)
        pass
        vcp(out=rowAb, in_=rowV)
        iv = vcp(out=rowBb, in_=rowS)
        c.inc(iv, "dve")
        pass
        c.wait(t, "dve")
        mm(out=p_a[:], lhsT=ones_rowb[:], rhs=rowAb, start=True, stop=True)
        im = mm(out=p_b[:], lhsT=ones_rowb[:], rhs=rowBb, start=True, stop=True)
        c.inc(im, "pe")
        c.wait(v, "pe")
        for jp in range(NCH // 2):
            kA, kB = 2 * jp, 2 * jp + 1
            c.wait(v, "ldx%d" % kA, ldx[kA])
            vtt(mxa[:], xv(kA), p_a[:], op=OP.mult)
            c.wait(v, "ldx%d" % kB, ldx[kB])
            vtt(S3[:], xv(kB), p_a[:], op=OP.mult)
            iv = vtt(mxb[:], mxa[:], p_b[:], op=OP.add)
            tA = c.counts["dve"]
            iv = vtt(S2[:], S3[:], p_b[:], op=OP.add)
            tB = c.counts["dve"]
            c.wait(s, "dve", tA)
            sact(out=h1_sb[:, kA * GRP:(kA + 1) * GRP], in_=mxb[:],
                 func=AF.Identity, scale=gmA[:, kA:kA + 1], bias=modt[:, kA:kA + 1])
            c.wait(s, "dve", tB)
            sact(out=h1_sb[:, kB * GRP:(kB + 1) * GRP], in_=S2[:],
                 func=AF.Identity, scale=gmA[:, kB:kB + 1], bias=modt[:, kB:kB + 1])
        identB_last = c.counts["act"]

        dveC0 = c.counts["dve"]
        thk = {}; vev = {}; s1d = {}; ped = {}
        for mc in range(NCH):
            pk, pv_ = (p_kr, p_v) if mc % 2 == 0 else (p_kr2, p_v2)
            c.wait(t, "act", thk[mc - 2] if mc >= 2 else identB_last)
            c.wait(t, "dve", vev[mc - 2] if mc >= 2 else dveC0)
            for kc in range(NCH):
                mm(out=pk[:], lhsT=wk_sb[:, kc * DM + mc * 128:kc * DM + (mc + 1) * 128],
                         rhs=h1_sb[:, kc * GRP:(kc + 1) * GRP],
                         start=(kc == 0), stop=(kc == NCH - 1))
            for kc in range(NCH):
                im = mm(out=pv_[:], lhsT=wv_sb[:, kc * DM + mc * 128:kc * DM + (mc + 1) * 128],
                              rhs=h1_sb[:, kc * GRP:(kc + 1) * GRP],
                              start=(kc == 0), stop=(kc == NCH - 1))
            c.inc(im, "pe")
            ped[mc] = c.counts["pe"]
            c.wait(s, "pe", ped[mc])
            c.wait(s, "dve", s1d[mc - 1] if mc >= 1 else dveC0)
            ia = sact(out=S1[:], in_=pk[:], func=AF.Identity, scale=10.0)
            c.inc(ia, "act")
            thk[mc] = c.counts["act"]
            c.wait(v, "pe", ped[mc])
            iv = vcp(out=v_bf, in_=pv_[:])
            c.inc(iv, "dve")
            vev[mc] = c.counts["dve"]
            c.wait(v, "act", thk[mc])
            vts(R1[:], jb_sb[:], f_sb[:, mc:mc + 1], t0f[:, mc:mc + 1],
                op0=OP.mult, op1=OP.add)
            sact(out=S2[:], in_=S1[:], func=AF.Identity, scale=I2PH)
            sact(out=R2[:], in_=R1[:], func=AF.Identity, scale=I2PH)
            vstt(S3[:], S1[:], I2PL, S2[:], op0=OP.mult, op1=OP.add)
            vstt(R3v, R1[:], I2PL, R2[:], op0=OP.mult, op1=OP.add)
            sact(out=S2[:], in_=S3[:], func=AF.Identity, bias=ceff[:])
            sact(out=R2[:], in_=R3v, func=AF.Identity, bias=ceff[:])
            sact(out=S3[:], in_=S2[:], func=AF.Identity, bias=nceff[:])
            sact(out=R3v, in_=R2[:], func=AF.Identity, bias=nceff[:])
            vstt(S2[:], S3[:], -TPH, S1[:], op0=OP.mult, op1=OP.add)
            vstt(R2[:], R3v, -TPH, R1[:], op0=OP.mult, op1=OP.add)
            vstt(S1[:], S3[:], -TPL, S2[:], op0=OP.mult, op1=OP.add)
            vstt(R1[:], R3v, -TPL, R2[:], op0=OP.mult, op1=OP.add)
            iv = vts(S2[:], S1[:], PI, -PI, op0=OP.min, op1=OP.max)
            c.inc(iv, "dve")
            s1d[mc] = c.counts["dve"]
            vts(R2[:], R1[:], PI, -PI, op0=OP.min, op1=OP.max)
            vstt(S3[:], S2[:], -1.0, S2[:], op0=OP.mult, op1=OP.max)
            iv = vstt(R3v, R2[:], -1.0, R2[:], op0=OP.mult, op1=OP.max)
            c.inc(iv, "dve")
            c.wait(s, "dve", c.counts["dve"])
            sact(out=skt[:, 1:GRP + 1], in_=S2[:], func=AF.Sin)
            sact(out=srt, in_=R2[:], func=AF.Sin)
            sact(out=ckt[:, 1:GRP + 1], in_=S3[:], func=AF.Sin,
                         scale=-1.0, bias=c.biasc(PIH))
            ia = sact(out=crt, in_=R3v, func=AF.Sin,
                              scale=-1.0, bias=c.biasc(PIH))
            c.inc(ia, "act")
            c.wait(v, "act", c.counts["act"])
            vcp(out=ckt[:, 0:1], in_=ckb[:, mc:mc + 1])
            vcp(out=skt[:, 0:1], in_=skb[:, mc:mc + 1])
            vcp(out=ckb[:, mc:mc + 1], in_=ckt[:, GRP:GRP + 1])
            vcp(out=skb[:, mc:mc + 1], in_=skt[:, GRP:GRP + 1])
            vtt(It0, v_bf, crt, op=OP.mult)
            vtt(It1, v_bf, srt, op=OP.mult)
            vscan(out=scA[:], data0=ones_gb[:], data1=It0,
                                 initial=carry[:, mc:mc + 1], op0=OP.mult, op1=OP.add)
            vscan(out=scB[:], data0=ones_gb[:], data1=It1,
                                 initial=carry[:, 8 + mc:9 + mc], op0=OP.mult, op1=OP.add)
            pb1 = gq1[:, :GRP]; pb2 = gq2[:, :GRP]; pb3 = gq1b[:, :GRP]
            vtt(It0, v_bf, ckt[:, 0:GRP], op=OP.mult)
            vcp(out=carry[:, mc:mc + 1], in_=scA[:, GRP - 1:GRP])
            vtt(pb1, scA[:], crt, op=OP.mult)
            vtt(It1, v_bf, skt[:, 0:GRP], op=OP.mult)
            vcp(out=carry[:, 8 + mc:9 + mc], in_=scB[:, GRP - 1:GRP])
            vtt(pb2, scB[:], srt, op=OP.mult)
            vscan(out=scA[:], data0=ones_gb[:], data1=It0,
                                 initial=carry[:, 16 + mc:17 + mc], op0=OP.mult, op1=OP.add)
            vtt(pb1, pb1, pb2, op=OP.add)
            vscan(out=scB[:], data0=ones_gb[:], data1=It1,
                                 initial=carry[:, 24 + mc:25 + mc], op0=OP.mult, op1=OP.add)
            vcp(out=carry[:, 16 + mc:17 + mc], in_=scA[:, GRP - 1:GRP])
            vtt(pb2, scA[:], ckt[:, 1:GRP + 1], op=OP.mult)
            vcp(out=carry[:, 24 + mc:25 + mc], in_=scB[:, GRP - 1:GRP])
            vtt(pb3, scB[:], skt[:, 1:GRP + 1], op=OP.mult)
            vts(pb1, pb1, gp_sb[:, mc:mc + 1], None, op0=OP.mult)
            vtt(pb2, pb2, pb3, op=OP.add)
            vstt(pb2, pb2, gac_sb[:, mc:mc + 1], pb1, op0=OP.mult, op1=OP.add)
            iv = vtt(mix_sb[:, mc * GRP:(mc + 1) * GRP], pb2, inv_sb[:], op=OP.mult)
            c.inc(iv, "dve")

        mixall = c.counts["dve"]
        evD = {}
        smmE = {}
        sqE = {}
        def emit_D(mc):
            pps = p_c if mc % 2 == 0 else p_kr
            c.wait(t, "dve", evD[mc - 2] if mc >= 2 else mixall)
            if mc < 2:
                c.wait(t, "act", thk[NCH - 1])
            for kc in range(NCH):
                im = mm(out=pps[:], lhsT=wo_sb[:, kc * DM + mc * 128:kc * DM + (mc + 1) * 128],
                              rhs=mix_sb[:, kc * GRP:(kc + 1) * GRP],
                              start=(kc == 0), stop=(kc == NCH - 1))
            c.inc(im, "pe")
            pedD = c.counts["pe"]
            c.wait(v, "pe", pedD)
            iv = vstt(xv(mc), pps[:], gA[:, mc:mc + 1], xv(mc), op0=OP.mult, op1=OP.add)
            c.inc(iv, "dve")
            evD[mc] = c.counts["dve"]
        def emit_stats(kc):
            sqt = scB if kc % 2 == 0 else scA
            c.wait(s, "dve", evD[kc])
            c.wait(s, "pe", smmE[kc - 2] if kc >= 2 else ped[NCH - 1])
            ia = sact(out=sqt[:], in_=xv(kc), func=AF.Square)
            c.inc(ia, "act")
            sqE[kc] = c.counts["act"]
            c.wait(t, "act", sqE[kc])
            c.wait(t, "dve", evD[kc])
            mm(out=p_sx, lhsT=ones_colf[:], rhs=xv(kc),
                     start=(kc == 0), stop=(kc == NCH - 1))
            im = mm(out=p_sq, lhsT=ones_colf[:], rhs=sqt[:],
                          start=(kc == 0), stop=(kc == NCH - 1))
            c.inc(im, "pe")
            smmE[kc] = c.counts["pe"]
        for mc in range(NCH):
            emit_D(mc)
            if mc >= 1:
                emit_stats(mc - 1)
        emit_stats(NCH - 1)
        c.wait(v, "pe")
        vts(rowS, p_sx, I1024, None, op0=OP.mult)
        pass
        vtt(rowT, rowS, rowS, op=OP.mult)
        pass
        iv = vstt(rowV, p_sq, I1024, rowT, op0=OP.mult, op1=OP.subtract)
        c.inc(iv, "dve")
        pass
        c.wait(s, "dve")
        ia = sact(out=rowT, in_=rowV, func=AF.Ln, bias=c.bias_tiles[EPS][0:1, :])
        c.inc(ia, "act")
        pass
        ia = sact(out=rowV, in_=rowT, func=AF.Exp, scale=-0.5)
        c.inc(ia, "act")
        c.wait(v, "act")
        vstt(rowS, rowS, -1.0, rowV, op0=OP.mult, op1=OP.mult)
        pass
        vcp(out=rowAb, in_=rowV)
        iv = vcp(out=rowBb, in_=rowS)
        c.inc(iv, "dve")
        pass
        c.wait(t, "dve")
        mm(out=p_a[:], lhsT=ones_rowb[:], rhs=rowAb, start=True, stop=True)
        im = mm(out=p_b[:], lhsT=ones_rowb[:], rhs=rowBb, start=True, stop=True)
        c.inc(im, "pe")
        c.wait(v, "pe")
        g1, g2 = gq1[:, :GRP], gq2[:, :GRP]
        g1b, g2b = gq1b[:, :GRP], gq2b[:, :GRP]
        g3, g4, g6 = gq3[:, :GRP], gq4[:, :GRP], gq6[:, :GRP]
        g3b, g4b, g6b = gq3b[:, :GRP], gq4b[:, :GRP], gq6b[:, :GRP]
        hrA_t = hr_sb[:, :GRP]; hrB_t = S1[:]
        for jp in range(NCH // 2):
            kA, kB = 2 * jp, 2 * jp + 1
            vtt(mxa[:], xv(kA), p_a[:], op=OP.mult)
            vtt(S3[:], xv(kB), p_a[:], op=OP.mult)
            iv = vtt(mxb[:], mxa[:], p_b[:], op=OP.add)
            t2A = c.counts["dve"]
            iv = vtt(S2[:], S3[:], p_b[:], op=OP.add)
            t2B = c.counts["dve"]
            c.wait(s, "dve", max(t2A, hrfree[0]))
            sact(out=hrA_t, in_=mxb[:], func=AF.Identity,
                 scale=gmB[:, kA:kA + 1], bias=modt[:, 24 + kA:25 + kA])
            idA = c.counts["act"]
            c.wait(s, "dve", max(t2B, hrfree[1]))
            sact(out=hrB_t, in_=S2[:], func=AF.Identity,
                 scale=gmB[:, kB:kB + 1], bias=modt[:, 24 + kB:25 + kB])
            idB = c.counts["act"]
            sact(out=g1, in_=hrA_t, func=AF.Square)
            sact(out=g1b, in_=hrB_t, func=AF.Square)
            sact(out=g2, in_=g1, func=AF.Ln, scale=0.5, bias=c.biasc(TINY2))
            lnA = c.counts["act"]
            sact(out=g2b, in_=g1b, func=AF.Ln, scale=0.5, bias=c.biasc(TINY2))
            lnB = c.counts["act"]
            h2A = h2q_sb[:, kA * GRP:(kA + 1) * GRP]
            h2B = h2q_sb[:, kB * GRP:(kB + 1) * GRP]
            c.wait(v, "act", idA)
            vcp(out=h2A, in_=hrA_t)
            c.wait(v, "act", idB)
            vcp(out=h2B, in_=hrB_t)
            c.wait(v, "act", lnA)
            vts(g3, g2, INV_LN2H, C52, op0=OP.mult, op1=OP.add)
            c.wait(v, "act", lnB)
            vts(g3b, g2b, INV_LN2H, C52, op0=OP.mult, op1=OP.add)
            vts(g1, g3, C52, -6.0, op0=OP.subtract, op1=OP.max)
            vts(g1b, g3b, C52, -6.0, op0=OP.subtract, op1=OP.max)
            iv = vts(g3, g1, 6.0, None, op0=OP.min)
            eA = c.counts["dve"]
            iv = vts(g3b, g1b, 6.0, None, op0=OP.min)
            eB = c.counts["dve"]
            c.wait(s, "dve", eA)
            sact(out=g2, in_=g3, func=AF.Exp, scale=LN2_HI, bias=c.biasc(BH14))
            exA = c.counts["act"]
            c.wait(s, "dve", eB)
            sact(out=g2b, in_=g3b, func=AF.Exp, scale=LN2_HI, bias=c.biasc(BH14))
            exB = c.counts["act"]
            c.wait(v, "act", exA)
            vts(g1, g2, C52, C52, op0=OP.add, op1=OP.subtract)
            c.wait(v, "act", exB)
            vts(g1b, g2b, C52, C52, op0=OP.add, op1=OP.subtract)
            vts(g4, g1, CS8, None, op0=OP.mult)
            vts(g4b, g1b, CS8, None, op0=OP.mult)
            vtt(g2, hrA_t, g4, op=OP.add)
            vtt(g2b, hrB_t, g4b, op=OP.add)
            vtt(g6, g2, g4, op=OP.subtract)
            vtt(g6b, g2b, g4b, op=OP.subtract)
            vtt(g4, g6, h2A, op=OP.subtract)
            vtt(g4b, g6b, h2B, op=OP.subtract)
            iv = vstt(h2A, g4, dcol[:], h2A, op0=OP.mult, op1=OP.add)
            hrfree[0] = c.counts["dve"]
            iv = vstt(h2B, g4b, dcol[:], h2B, op0=OP.mult, op1=OP.add)
            hrfree[1] = c.counts["dve"]

        dveF0 = c.counts["dve"]
        actF0 = c.counts["act"]
        reluF = {}
        hrA_t = h16_sb[:, :GRP]; hrB_t = h16_sb[:, GRP:2 * GRP]
        g3, g4, g6 = gq3[:, :GRP], gq4[:, :GRP], gq6[:, :GRP]
        g3b, g4b, g6b = gq3b[:, :GRP], gq4b[:, :GRP], gq6b[:, :GRP]
        for j in range(NHF // 2):
            hA, hB = 2 * j, 2 * j + 1
            pA_, pB_ = (p_a, p_b) if j % 2 == 0 else (p_kr, p_kr2)
            c.wait(t, "act", reluF[hA - 4] if hA >= 4 else actF0)
            if j == 0:
                c.wait(t, "dve", dveF0)
            for kc in range(NCH):
                im = mm(out=pA_[:],
                              lhsT=w1q_sb[:, kc * HF + hA * 128:kc * HF + (hA + 1) * 128],
                              rhs=h2q_sb[:, kc * GRP:(kc + 1) * GRP],
                              start=(kc == 0), stop=(kc == NCH - 1))
            c.inc(im, "pe"); peA = c.counts["pe"]
            c.wait(t, "act", reluF[hB - 4] if hB >= 4 else actF0)
            for kc in range(NCH):
                im = mm(out=pB_[:],
                              lhsT=w1q_sb[:, kc * HF + hB * 128:kc * HF + (hB + 1) * 128],
                              rhs=h2q_sb[:, kc * GRP:(kc + 1) * GRP],
                              start=(kc == 0), stop=(kc == NCH - 1))
            c.inc(im, "pe"); peB = c.counts["pe"]
            c.wait(s, "pe", peA)
            c.wait(s, "dve", hrfree[0])
            ia = sact(out=hrA_t, in_=pA_[:], func=AF.Relu, bias=b1q_sb[:, hA:hA + 1])
            c.inc(ia, "act"); reluF[hA] = c.counts["act"]
            c.wait(s, "pe", peB)
            c.wait(s, "dve", hrfree[1])
            ia = sact(out=hrB_t, in_=pB_[:], func=AF.Relu, bias=b1q_sb[:, hB:hB + 1])
            c.inc(ia, "act"); reluF[hB] = c.counts["act"]
            rslA = r_sb[:, hA * GRP:(hA + 1) * GRP]
            rslB = r_sb[:, hB * GRP:(hB + 1) * GRP]
            g1, g2 = gq1[:, :GRP], gq2[:, :GRP]
            g1b, g2b = gq1b[:, :GRP], gq2b[:, :GRP]
            ia = sact(out=g1, in_=hrA_t, func=AF.Square)
            c.inc(ia, "act")
            ia = sact(out=g1b, in_=hrB_t, func=AF.Square)
            c.inc(ia, "act")
            ia = sact(out=g2, in_=g1, func=AF.Ln, scale=0.5, bias=c.biasc(TINY2))
            c.inc(ia, "act"); lnA = c.counts["act"]
            ia = sact(out=g2b, in_=g1b, func=AF.Ln, scale=0.5, bias=c.biasc(TINY2))
            c.inc(ia, "act"); lnB = c.counts["act"]
            c.wait(v, "act", lnA)
            vts(g3, g2, INV_LN2H, C52, op0=OP.mult, op1=OP.add)
            c.wait(v, "act", lnB)
            vts(g3b, g2b, INV_LN2H, C52, op0=OP.mult, op1=OP.add)
            vts(g1, g3, C52, -14.0, op0=OP.subtract, op1=OP.max)
            vts(g1b, g3b, C52, -14.0, op0=OP.subtract, op1=OP.max)
            iv = vts(g3, g1, 5.0, None, op0=OP.min)
            c.inc(iv, "dve"); eA = c.counts["dve"]
            iv = vts(g3b, g1b, 5.0, None, op0=OP.min)
            c.inc(iv, "dve"); eB = c.counts["dve"]
            c.wait(s, "dve", eA)
            ia = sact(out=g2, in_=g3, func=AF.Exp, scale=LN2_HI, bias=c.biasc(BH14))
            c.inc(ia, "act"); exA = c.counts["act"]
            c.wait(s, "dve", eB)
            ia = sact(out=g2b, in_=g3b, func=AF.Exp, scale=LN2_HI, bias=c.biasc(BH14))
            c.inc(ia, "act"); exB = c.counts["act"]
            c.wait(v, "act", exA)
            vts(g1, g2, C52, C52, op0=OP.add, op1=OP.subtract)
            c.wait(v, "act", exB)
            vts(g1b, g2b, C52, C52, op0=OP.add, op1=OP.subtract)
            vts(g4, g1, CS16, None, op0=OP.mult)
            vts(g4b, g1b, CS16, None, op0=OP.mult)
            vtt(g2, hrA_t, g4, op=OP.add)
            vtt(g2b, hrB_t, g4b, op=OP.add)
            vtt(g6, g2, g4, op=OP.subtract)
            vtt(g6b, g2b, g4b, op=OP.subtract)
            c.wait(v, "act", reluF[hB])
            vcp(out=rslA, in_=hrA_t)
            vcp(out=rslB, in_=hrB_t)
            vts(g3, g1, 256.0, None, op0=OP.max)
            vts(g3b, g1b, 256.0, None, op0=OP.max)
            vts(g4, g3, CS8, None, op0=OP.mult)
            vts(g4b, g3b, CS8, None, op0=OP.mult)
            vtt(g1, g6, g4, op=OP.add)
            vtt(g1b, g6b, g4b, op=OP.add)
            vtt(g2, g1, g4, op=OP.subtract)
            vtt(g2b, g1b, g4b, op=OP.subtract)
            vtt(g4, g2, rslA, op=OP.subtract)
            vtt(g4b, g2b, rslB, op=OP.subtract)
            iv = vstt(rslA, g4, dcol[:], rslA, op0=OP.mult, op1=OP.add)
            c.inc(iv, "dve"); hrfree[0] = c.counts["dve"]
            iv = vstt(rslB, g4b, dcol[:], rslB, op0=OP.mult, op1=OP.add)
            c.inc(iv, "dve"); hrfree[1] = c.counts["dve"]

        rall_d = c.counts["dve"]
        y16A = h16_sb[:, :GRP]; y16B = h16_sb[:, GRP:2 * GRP]
        mblA = ckt[:, 0:GRP]; mblB = skt[:, 0:GRP]
        for jp in range(NCH // 2):
            mA_, mB_ = 2 * jp, 2 * jp + 1
            c.wait(sy, "ld", c.counts["ld"])
            c.wait(sy, "out", c.counts["out"])
            dma(mkA[:], mA_r[:, mA_, gsl], "ld")
            dma(mkD[:], mD_r[:, mA_, gsl], "ld")
            dma(mkA2[:], mA_r[:, mB_, gsl], "out")
            dma(mkD2[:], mD_r[:, mB_, gsl], "out")
            ldm = c.counts["ld"]
            ldm2 = c.counts["out"]
            if jp == 0:
                c.wait(t, "dve", rall_d)
            for kc in range(NHF):
                mm(out=p_c[:],
                   lhsT=w2q_sb[:, kc * DM + mA_ * 128:kc * DM + (mA_ + 1) * 128],
                   rhs=r_sb[:, kc * GRP:(kc + 1) * GRP],
                   start=(kc == 0), stop=(kc == NHF - 1))
            peA = c.counts["pe"]
            for kc in range(NHF):
                mm(out=p_a[:],
                   lhsT=w2q_sb[:, kc * DM + mB_ * 128:kc * DM + (mB_ + 1) * 128],
                   rhs=r_sb[:, kc * GRP:(kc + 1) * GRP],
                   start=(kc == 0), stop=(kc == NHF - 1))
            peB = c.counts["pe"]
            c.wait(s, "pe", peA)
            sact(out=y16A, in_=p_c[:], func=AF.Identity, bias=b2q_sb[:, mA_:mA_ + 1])
            yidA = c.counts["act"]
            c.wait(s, "pe", peB)
            sact(out=y16B, in_=p_a[:], func=AF.Identity, bias=b2q_sb[:, mB_:mB_ + 1])
            yidB = c.counts["act"]
            sact(out=gq1[:, :GRP], in_=y16A, func=AF.Square)
            sact(out=gq1b[:, :GRP], in_=y16B, func=AF.Square)
            sact(out=gq2[:, :GRP], in_=gq1[:, :GRP], func=AF.Ln, scale=0.5, bias=c.biasc(TINY2))
            lnA = c.counts["act"]
            sact(out=gq2b[:, :GRP], in_=gq1b[:, :GRP], func=AF.Ln, scale=0.5, bias=c.biasc(TINY2))
            lnB = c.counts["act"]
            g1, g2 = gq1[:, :GRP], gq2[:, :GRP]
            g1b, g2b = gq1b[:, :GRP], gq2b[:, :GRP]
            g3, g4, g6 = gq3[:, :GRP], gq4[:, :GRP], gq6[:, :GRP]
            g3b, g4b, g6b = gq3b[:, :GRP], gq4b[:, :GRP], gq6b[:, :GRP]
            c.wait(v, "ld", ldm)
            vstt(mblA, mkD[:], dcol[:], mkA[:], op0=OP.mult, op1=OP.add)
            c.wait(v, "out", ldm2)
            vstt(mblB, mkD2[:], dcol[:], mkA2[:], op0=OP.mult, op1=OP.add)
            c.wait(v, "act", lnA)
            vts(g3, g2, INV_LN2H, C52, op0=OP.mult, op1=OP.add)
            c.wait(v, "act", lnB)
            vts(g3b, g2b, INV_LN2H, C52, op0=OP.mult, op1=OP.add)
            vts(g1, g3, C52, -14.0, op0=OP.subtract, op1=OP.max)
            vts(g1b, g3b, C52, -14.0, op0=OP.subtract, op1=OP.max)
            iv = vts(g3, g1, 6.0, None, op0=OP.min)
            eA = c.counts["dve"]
            iv = vts(g3b, g1b, 6.0, None, op0=OP.min)
            eB = c.counts["dve"]
            c.wait(s, "dve", eA)
            sact(out=g2, in_=g3, func=AF.Exp, scale=LN2_HI, bias=c.biasc(BH14))
            exA = c.counts["act"]
            c.wait(s, "dve", eB)
            sact(out=g2b, in_=g3b, func=AF.Exp, scale=LN2_HI, bias=c.biasc(BH14))
            exB = c.counts["act"]
            c.wait(v, "act", exA)
            vts(g1, g2, C52, C52, op0=OP.add, op1=OP.subtract)
            c.wait(v, "act", exB)
            vts(g1b, g2b, C52, C52, op0=OP.add, op1=OP.subtract)
            vts(g4, g1, CS16, None, op0=OP.mult)
            vts(g4b, g1b, CS16, None, op0=OP.mult)
            vtt(g2, y16A, g4, op=OP.add)
            vtt(g2b, y16B, g4b, op=OP.add)
            vtt(g6, g2, g4, op=OP.subtract)
            vtt(g6b, g2b, g4b, op=OP.subtract)
            vtt(g4, g6, y16A, op=OP.subtract)
            vtt(g4b, g6b, y16B, op=OP.subtract)
            vstt(scA[:], g4, dcol[:], y16A, op0=OP.mult, op1=OP.add)
            vstt(scB[:], g4b, dcol[:], y16B, op0=OP.mult, op1=OP.add)
            vtt(mxa[:], scA[:], mblA, op=OP.mult)
            vtt(S3[:], scB[:], mblB, op=OP.mult)
            iv = vstt(xv(mA_), mxa[:], gB[:, mA_:mA_ + 1], xv(mA_), op0=OP.mult, op1=OP.add)
            iv = vstt(xv(mB_), S3[:], gB[:, mB_:mB_ + 1], xv(mB_), op0=OP.mult, op1=OP.add)
            prev_scA_free = c.counts["dve"]
            prev_scB_free = c.counts["dve"]
            dma(yout_r[:, mA_, gsl], xv(mA_), "ldx%d" % mA_)
            prev_out[mA_] = c.counts["ldx%d" % mA_]
            dma(yout_r[:, mB_, gsl], xv(mB_), "ldx%d" % mB_)
            prev_out[mB_] = c.counts["ldx%d" % mB_]

    for kc in range(NCH):
        c.wait(sy, "ldx%d" % kc)
    c.wait(sy, "out")
    if debug:
        dbg = {"d_modt": (modt, F32, [128, 48]), "d_gA": (gA, F32, [128, 8]),
               "d_gB": (gB, F32, [128, 8]), "d_gmA": (gmA, F32, [128, 8]),
               "d_dcol": (dcol, F32, [128, 1]), "d_ceff": (ceff, F32, [128, 1]),
               "d_h1": (h1_sb, BF16, [128, NCH * GRP]),
               "d_mix": (mix_sb, BF16, [128, NCH * GRP]),
               "d_h2q": (h2q_sb, FP8, [128, NCH * GRP]),
               "d_rsb": (r_sb, FP8, [128, NHF * GRP]),
               "d_carry": (carry, F32, [128, 32]),
               "d_b1q": (b1q_sb, F32, [128, 32]),
               "d_w1q": (w1q_sb, FP8, [128, NCH * HF]),
               "d_ckb": (ckb, BF16, [128, 8])}
        for nm, (tile, dt, sh) in dbg.items():
            od = nc.dram_tensor(nm, sh, dt, kind="ExternalOutput").ap()
            dma(od, tile[:], "out")
        c.wait(sy, "out")
    stack.close()
    return nc


def kernel(L0_attn_x, L0_attn_Wk, L0_attn_Wv, L0_attn_Wo, L0_attn_gate,
           L0_attn_freqs, L0_ffn_w1, L0_ffn_b1, L0_ffn_w2, L0_ffn_b2,
           L0_c, L0_wmod):
    asc = np.ascontiguousarray
    maskA, maskD = host_masks()

    Wk = asc(L0_attn_Wk, dtype=np.float32)
    Wv = asc(L0_attn_Wv, dtype=np.float32)
    Wo = asc(L0_attn_Wo, dtype=np.float32)
    w1T = asc(np.float32(L0_ffn_w1).T)
    w2T = asc(np.float32(L0_ffn_w2).T)
    wmod = asc(L0_wmod, dtype=np.float32)
    fr = asc(L0_attn_freqs, dtype=np.float32).reshape(-1)
    f_col = asc(fr.reshape(8, 128).T)
    gp_col = asc(np.repeat(np.float32(L0_attn_gate[:, 0]), 64).reshape(8, 128).T)
    gac_col = asc(np.repeat(np.float32(L0_attn_gate[:, 1]), 64).reshape(8, 128).T)
    b1_col = asc(np.float32(L0_ffn_b1).reshape(32, 128).T)
    b2_col = asc(np.float32(L0_ffn_b2).reshape(8, 128).T)
    jb = asc(np.broadcast_to(np.arange(GRP, dtype=np.float32)[None, :], (128, GRP)))
    s32 = np.sqrt(np.arange(1, T + 1, dtype=np.float32))
    inv = (np.float32(1.0) / s32).astype(np.float32)
    invs = asc(np.broadcast_to(inv[None, :], (128, T)))

    nc = build(T)
    in_maps = []
    for b in range(8):
        in_maps.append({
            "xT": asc(np.float32(L0_attn_x[b]).T),
            "Wk": Wk, "Wv": Wv, "Wo": Wo,
            "w1T": w1T, "w2T": w2T, "wmod": wmod,
            "c_col": asc(np.float32(L0_c[b]).reshape(8, 128).T),
            "f_col": f_col, "gp_col": gp_col, "gac_col": gac_col,
            "b1_col": b1_col, "b2_col": b2_col,
            "jb": jb, "invs": invs,
            "maskA": asc(maskA[b].T),
            "maskD": asc(maskD[b].T),
        })
    res = run_bass_kernel_spmd(nc, in_maps, core_ids=list(range(8)))
    out = np.empty((8, T, DM), dtype=np.float32)
    for b in range(8):
        out[b] = np.asarray(res.results[b]["y"]).T
    return out
